# version 14 (final)
# baseline (speedup 1.0000x reference)
import numpy as np
import ml_dtypes
import concourse.bass as bass
import concourse.mybir as mybir
from concourse.bass_utils import run_bass_kernel_spmd

F32 = mybir.dt.float32
BF16 = mybir.dt.bfloat16
AF = mybir.ActivationFunctionType
ALU = mybir.AluOpType
AX = mybir.AxisListType.X

T = 2048
D = 2048
NTT = 16
LAMBDA_INIT = 0.8 - 0.6 * float(np.exp(-0.3 * 12))
SCL = float(128.0 ** 0.5)

OPS = []
_sem = [0, 0]

def op(eng, fn, inc=1):
    waits = [(_sem[0], _sem[1])]
    if _sem[1] > 50000:
        _sem[0] += 1
        _sem[1] = 0
    OPS.append((eng, fn, waits, _sem[0], inc))
    _sem[1] += inc

def kernel(**inputs):
    OPS.clear()
    _sem[0] = 0
    _sem[1] = 0
    x = inputs['L0_attn_a_x']
    WQA = np.ascontiguousarray(inputs['L0_attn_a_Wq'].T)
    WKA = np.ascontiguousarray(inputs['L0_attn_a_Wk'].T)
    WVA = np.ascontiguousarray(inputs['L0_attn_a_Wv'].T)
    WOA = np.ascontiguousarray(inputs['L0_attn_a_Wo'].T)
    nw = np.ascontiguousarray(inputs['L0_attn_a_norm_w'].reshape(1, D))
    perm = np.concatenate([h * 64 + np.concatenate([np.arange(0, 64, 2), np.arange(1, 64, 2)]) for h in range(32)])
    WQB = np.ascontiguousarray(inputs['L0_attn_b_wq'][perm, :].T)
    WKB = np.ascontiguousarray(inputs['L0_attn_b_wk'][perm, :].T)
    WVB = np.ascontiguousarray(inputs['L0_attn_b_wv'].T)
    WOB = np.ascontiguousarray(inputs['L0_attn_b_wout'].T)
    lq1 = np.ascontiguousarray(inputs['L0_attn_b_lambda_q1'].reshape(1, 64))
    lq2 = np.ascontiguousarray(inputs['L0_attn_b_lambda_q2'].reshape(1, 64))
    lk1 = np.ascontiguousarray(inputs['L0_attn_b_lambda_k1'].reshape(1, 64))
    lk2 = np.ascontiguousarray(inputs['L0_attn_b_lambda_k2'].reshape(1, 64))
    subw = np.ascontiguousarray(inputs['L0_attn_b_subln_weight'].reshape(128, 1))
    WGU = np.ascontiguousarray(inputs['L0_ffn_w_gate_up'].T)
    WDN = np.ascontiguousarray(inputs['L0_ffn_w_down'].T)

    U64 = np.triu(np.ones((64, 64), dtype=np.float32))
    O64 = np.ones((64, 1), dtype=np.float32)
    O1x64 = np.ones((1, 64), dtype=np.float32)
    O128 = np.ones((128, 1), dtype=np.float32)
    O1x128 = np.ones((1, 128), dtype=np.float32)
    OB128 = np.ones((128, 1), dtype=ml_dtypes.bfloat16)
    IDB = np.eye(128, dtype=ml_dtypes.bfloat16)
    trU = np.triu(np.ones((64, 64), dtype=np.float32))
    MASKT = np.concatenate([trU, trU], axis=1)
    inv = 1.0 / 10000.0 ** (np.arange(0, 64, 2)[:32].astype(np.float64) / 64)

    nc = bass.Bass("TRN2", target_bir_lowering=False)
    def din(name, shape, dt=F32):
        return nc.dram_tensor(name, list(shape), dt, kind="ExternalInput").ap()
    def dint(name, shape, dt=F32, aspace=None):
        if aspace is None:
            return nc.dram_tensor(name, list(shape), dt, kind="Internal").ap()
        return nc.dram_tensor(name, list(shape), dt, kind="Internal", addr_space=aspace).ap()
    xin = din("x", (T, D))
    wqa = din("wqa", (D, D)); wka = din("wka", (D, D)); wva = din("wva", (D, D)); woa = din("woa", (D, D))
    nwt = din("nw", (1, D))
    wqb = din("wqb", (D, D)); wkb = din("wkb", (D, D)); wvb = din("wvb", (D, D)); wob = din("wob", (D, D))
    lq1t = din("lq1", (1, 64)); lq2t = din("lq2", (1, 64)); lk1t = din("lk1", (1, 64)); lk2t = din("lk2", (1, 64))
    subt = din("subw", (128, 1))
    wgu = din("wgu", (D, 16384)); wdn = din("wdn", (8192, D))
    u64t = din("u64", (64, 64)); o64t = din("o64", (64, 1)); o1x64t = din("o1x64", (1, 64))
    o128t = din("o128", (128, 1)); o1x128t = din("o1x128", (1, 128))
    ob128t = din("ob128", (128, 1), BF16)
    idbt = din("idb", (128, 128), BF16)
    maskt = din("maskt", (64, 128))
    cost = din("cosk", (128, T)); sint = din("sink", (128, T))
    wselt = din("wsel", (128, 1)); zselt = din("zsel", (128, 1))
    cvalst = din("cvals", (128, 16))
    pmaskt = din("pmask", (128, 8))
    pmaskbt = din("pmaskb", (128, 8), BF16)
    s0in = din("s0in", (128, D))
    kvpart = din("kvpart", (2 * T, D), BF16)
    yout = nc.dram_tensor("y", [T, D], F32, kind="ExternalOutput").ap()
    fout = nc.dram_tensor("fout", [128, D], F32, kind="ExternalOutput").ap()
    kvout = nc.dram_tensor("kvout", [2 * T, D], BF16, kind="ExternalOutput").ap()

    XT = dint("XT", (D, T), BF16)
    QSd = dint("QSd", (T, D)); LFd = dint("LFd", (T, D)); K1d = dint("K1d", (T, D))
    VAd = dint("VAd", (T, D), BF16)
    OAd = dint("OAd", (T, D))
    QDTd = dint("QDTd", (D, T), BF16)
    LPCd = dint("LPCd", (32 * 128, 16))

    OATd = dint("OATd", (D, T), BF16)
    Rd = dint("Rd", (T, D))
    QTRd = dint("QTRd", (D, T), BF16)
    KVI = dint("KVI", (2 * T, D), BF16)

    HTd = dint("HTd", (8192, T), BF16)
    OTBd = dint("OTBd", (D, T), BF16)
    WGUB = dint("WGUB", (D, 16384), BF16)
    WDNB = dint("WDNB", (8192, D), BF16)

    import contextlib
    with contextlib.ExitStack() as st:
        E = st.enter_context
        XTS = E(nc.sbuf_tensor("XTS", [128, 32768], BF16))
        G1 = E(nc.sbuf_tensor("G1", [128, 2048], F32))
        G2 = E(nc.sbuf_tensor("G2", [128, 2048], F32))
        G3 = E(nc.sbuf_tensor("G3", [128, 2048], F32))
        B1 = E(nc.sbuf_tensor("B1", [128, 2048], BF16))
        B2 = E(nc.sbuf_tensor("B2", [128, 2048], BF16))
        WS1 = E(nc.sbuf_tensor("WS1", [128, 1024], F32))
        WB1 = E(nc.sbuf_tensor("WB1", [128, 1024], BF16))
        WL = E(nc.sbuf_tensor("WL", [128, 2048], BF16))
        WL2 = E(nc.sbuf_tensor("WL2", [128, 2048], BF16))
        EP = E(nc.sbuf_tensor("EP", [128, 2048], F32))
        EN = EP
        MNW = EP
        QD = E(nc.sbuf_tensor("QD", [128, 2048], BF16))
        KD16 = E(nc.sbuf_tensor("KD16", [128, 2048], BF16))
        KR = E(nc.sbuf_tensor("KR", [128, 2048], BF16))
        QDT = E(nc.sbuf_tensor("QDT", [128, 2048], BF16))
        KDT = E(nc.sbuf_tensor("KDT", [128, 2048], BF16))
        SST = E(nc.sbuf_tensor("SST", [128, 2048], F32))
        SB16 = E(nc.sbuf_tensor("SB16", [128, 2048], BF16))
        OTI = E(nc.sbuf_tensor("OTI", [128, 2048], F32))
        EAC = E(nc.sbuf_tensor("EAC", [128, 32], F32))
        LAC = E(nc.sbuf_tensor("LAC", [128, 32], F32))
        LPc = E(nc.sbuf_tensor("LPc", [128, 16], F32))
        PCS = E(nc.sbuf_tensor("PCS", [128, 16], F32))
        SCT = E(nc.sbuf_tensor("SCT", [64, 128], BF16))
        LFC = E(nc.sbuf_tensor("LFC", [64, 2048], F32))
        VC1 = E(nc.sbuf_tensor("VC1", [64, 2048], BF16))
        SROW = E(nc.sbuf_tensor("SROW", [1, 2048], F32))
        RWS = E(nc.sbuf_tensor("RWS", [1, 2048], F32))
        STT = E(nc.sbuf_tensor("STT", [128, 8], F32))
        T128 = E(nc.sbuf_tensor("T128", [128, 128], F32))
        LAMC = E(nc.sbuf_tensor("LAMC", [128, 1], F32))
        cU64 = E(nc.sbuf_tensor("cU64", [64, 64], F32))
        cO64 = E(nc.sbuf_tensor("cO64", [64, 1], F32))
        cO1x64 = E(nc.sbuf_tensor("cO1x64", [1, 64], F32))
        cO128 = E(nc.sbuf_tensor("cO128", [128, 1], F32))
        cO1x128 = E(nc.sbuf_tensor("cO1x128", [1, 128], F32))
        cOB128 = E(nc.sbuf_tensor("cOB128", [128, 1], BF16))
        cIDB = E(nc.sbuf_tensor("cIDB", [128, 128], BF16))
        cMASK = E(nc.sbuf_tensor("cMASK", [64, 128], F32))
        cCOS = E(nc.sbuf_tensor("cCOS", [128, 2048], F32))
        cSIN = E(nc.sbuf_tensor("cSIN", [128, 2048], F32))
        cSUB = E(nc.sbuf_tensor("cSUB", [128, 1], F32))
        cWSEL = E(nc.sbuf_tensor("cWSEL", [128, 1], F32))
        cZSEL = E(nc.sbuf_tensor("cZSEL", [128, 1], F32))
        cLQ1 = E(nc.sbuf_tensor("cLQ1", [1, 64], F32))
        cLQ2 = E(nc.sbuf_tensor("cLQ2", [1, 64], F32))
        cLK1 = E(nc.sbuf_tensor("cLK1", [1, 64], F32))
        cLK2 = E(nc.sbuf_tensor("cLK2", [1, 64], F32))
        cCV = E(nc.sbuf_tensor("cCV", [128, 16], F32))
        cPM = E(nc.sbuf_tensor("cPM", [128, 8], F32))
        cPMB = E(nc.sbuf_tensor("cPMB", [128, 8], BF16))
        PBa = E(nc.psum_tensor("PBa", [128, 1024], F32))
        PBb = E(nc.psum_tensor("PBb", [128, 1024], F32))
        PBc = E(nc.psum_tensor("PBc", [128, 1024], F32))
        PB4 = E(nc.psum_tensor("PB4", [128, 512], F32))
        PT = E(nc.psum_tensor("PT", [128, 128], BF16))
        SEMS = [E(nc.semaphore("s%d" % i)) for i in range(8)]
        CONSTVALS = [1.0 / 2048.0, -1.0, 1e-5, 1e-6, LAMBDA_INIT, 1.0 / 128.0, SCL, 1.0, 0.0, 0.5, 2048.0, 128.0]
        for _i, _v in enumerate(CONSTVALS):
            nc.const_aps.register(F32, _v, cCV[:, _i:_i + 1])

        def DMA(dst, src):
            op('sy', lambda d=dst, s=src: nc.sync.dma_start(out=d, in_=s), inc=16)
        def TMM(out, lhsT, rhs, start=True, stop=True):
            w = rhs.shape[-1]
            if w > 512:
                h = w // 2
                TMM(out[:, 0:h], lhsT, rhs[:, 0:h], start=start, stop=stop)
                TMM(out[:, h:w], lhsT, rhs[:, h:w], start=start, stop=stop)
                return
            op('te', lambda o=out, l=lhsT, r=rhs, s=start, p=stop: nc.tensor.matmul(o, l, r, start=s, stop=p))
        def TTR(out, in_):
            op('te', lambda o=out, i=in_: nc.tensor.transpose(o, i, cIDB[:, :]))
        def VTT(out, a, b, alu):
            op('ve', lambda o=out, x=a, y=b, u=alu: nc.vector.tensor_tensor(o, x, y, u))
        def VCP(out, in_):
            op('ve', lambda o=out, i=in_: nc.vector.tensor_copy(o, i))
        def VTS(out, in_, s1, s2, op0, op1=None):
            if op1 is None:
                op('ve', lambda o=out, i=in_, a=s1, u=op0: nc.vector.tensor_scalar(o, i, a, None, u))
            else:
                op('ve', lambda o=out, i=in_, a=s1, b=s2, u=op0, v=op1: nc.vector.tensor_scalar(o, i, a, b, u, v))
        def VRS(out, in_):
            op('ve', lambda o=out, i=in_: nc.vector.reduce_sum(o, i, axis=AX))
        def VRC(out, in_):
            op('ve', lambda o=out, i=in_: nc.vector.reciprocal(o, i))
        def VMS(ap, c):
            op('ve', lambda a=ap, k=c: nc.vector.memset(a, k))
        def ACT(out, in_, func, scale=None):
            if scale is None:
                op('se', lambda o=out, i=in_, f=func: nc.scalar.activation(o, i, f))
            else:
                op('se', lambda o=out, i=in_, f=func, s=scale: nc.scalar.activation(o, i, f, scale=s))
        def SMUL(out, in_, c):
            op('se', lambda o=out, i=in_, k=c: nc.scalar.mul(o, i, k))
        def SADD(out, in_, c):
            op('se', lambda o=out, i=in_, k=c: nc.scalar.add(o, i, k))
        def SSQ(out, in_):
            op('se', lambda o=out, i=in_: nc.scalar.square(o, i))
        def SSQRT(out, in_):
            op('se', lambda o=out, i=in_: nc.scalar.sqrt(o, i))
        def SCP(out, in_):
            op('se', lambda o=out, i=in_: nc.scalar.copy(o, i))

        for dst, src in [(cCV, cvalst), (cPM, pmaskt), (cPMB, pmaskbt), (cU64, u64t), (cO64, o64t), (cO1x64, o1x64t), (cO128, o128t),
                         (cO1x128, o1x128t), (cOB128, ob128t), (cIDB, idbt), (cMASK, maskt),
                         (cCOS, cost), (cSIN, sint), (cSUB, subt), (cWSEL, wselt), (cZSEL, zselt),
                         (cLQ1, lq1t), (cLQ2, lq2t), (cLK1, lk1t), (cLK2, lk2t)]:
            DMA(dst[:, :], src[:, :])
        VTT(SROW[0:1, 0:64], cLQ1[:, :], cLK1[:, :], ALU.mult)
        VRS(SROW[0:1, 64:65], SROW[0:1, 0:64])
        ACT(SROW[0:1, 65:66], SROW[0:1, 64:65], AF.Exp)
        VTT(SROW[0:1, 0:64], cLQ2[:, :], cLK2[:, :], ALU.mult)
        VRS(SROW[0:1, 66:67], SROW[0:1, 0:64])
        ACT(SROW[0:1, 67:68], SROW[0:1, 66:67], AF.Exp)
        VTT(SROW[0:1, 68:69], SROW[0:1, 65:66], SROW[0:1, 67:68], ALU.subtract)
        SADD(SROW[0:1, 69:70], SROW[0:1, 68:69], LAMBDA_INIT)
        TMM(PB4[0:128, 0:1], cO1x128[:, :], SROW[0:1, 69:70])
        VCP(LAMC[:, :], PB4[0:128, 0:1])
        for r in range(16):
            for c in range(8):
                DMA(G1[:, :], wgu[r * 128:(r + 1) * 128, c * 2048:(c + 1) * 2048])
                VCP(B1[:, :], G1[:, :])
                DMA(WGUB[r * 128:(r + 1) * 128, c * 2048:(c + 1) * 2048], B1[:, :])
        for r in range(64):
            DMA(G1[:, :], wdn[r * 128:(r + 1) * 128, :])
            VCP(B1[:, :], G1[:, :])
            DMA(WDNB[r * 128:(r + 1) * 128, :], B1[:, :])

        def LN(src, eps=1e-5):
            for tt in range(NTT):
                r0 = tt * 128
                DMA(G1[:, :], src[r0:r0 + 128, :])
                VRS(STT[:, 0:1], G1[:, :])
                SSQ(G2[:, :], G1[:, :])
                VRS(STT[:, 1:2], G2[:, :])
                SMUL(STT[:, 2:3], STT[:, 0:1], 1.0 / D)
                SMUL(STT[:, 3:4], STT[:, 1:2], 1.0 / D)
                SSQ(STT[:, 4:5], STT[:, 2:3])
                VTT(STT[:, 5:6], STT[:, 3:4], STT[:, 4:5], ALU.subtract)
                SADD(STT[:, 6:7], STT[:, 5:6], eps)
                SSQRT(STT[:, 7:8], STT[:, 6:7])
                VRC(STT[:, 0:1], STT[:, 7:8])
                SMUL(STT[:, 1:2], STT[:, 2:3], -1.0)
                VTS(B1[:, :], G1[:, :], STT[:, 1:2], STT[:, 0:1], ALU.add, ALU.mult)
                for kb in range(16):
                    TTR(PT[:, :], B1[:, kb * 128:(kb + 1) * 128])
                    VCP(B2[:, 0:128], PT[:, :])
                    DMA(XT[kb * 128:(kb + 1) * 128, r0:r0 + 128], B2[:, 0:128])

        def tok_linear(srcT, Kdim, W, wbf, Nout, evac):
            KB = Kdim // 128
            groups = [[0, 1, 2], [3, 4, 5], [6, 7, 8], [9, 10, 11], [12, 13, 14], [15]]
            PBq = [PBa, PBb, PBc]
            for g in groups:
                G = len(g)
                g0 = g[0]
                for kb in range(KB):
                    DMA(XTS[:, kb * G * 128:kb * G * 128 + G * 128],
                        srcT[kb * 128:(kb + 1) * 128, g0 * 128:g0 * 128 + G * 128])
                for n in range(Nout // 1024):
                    for kb in range(KB):
                        if wbf:
                            DMA(WB1[:, :], W[kb * 128:(kb + 1) * 128, n * 1024:(n + 1) * 1024])
                        else:
                            DMA(WS1[:, :], W[kb * 128:(kb + 1) * 128, n * 1024:(n + 1) * 1024])
                            VCP(WB1[:, :], WS1[:, :])
                        for i in range(G):
                            TMM(PBq[i][:, :], XTS[:, kb * G * 128 + i * 128:kb * G * 128 + (i + 1) * 128],
                                WB1[:, :], start=(kb == 0), stop=(kb == KB - 1))
                    for i in range(G):
                        evac(g[i], n, PBq[i])

        def ev_qa(t, n, ps):
            ACT(G1[:, 0:1024], ps[:, :], AF.Silu)
            DMA(QSd[t * 128:(t + 1) * 128, n * 1024:(n + 1) * 1024], G1[:, 0:1024])
        def ev_fa(t, n, ps):
            ACT(G1[:, 0:1024], ps[:, :], AF.Sigmoid)
            ACT(G2[:, 0:1024], G1[:, 0:1024], AF.Sigmoid)
            ACT(G3[:, 0:1024], G2[:, 0:1024], AF.Ln)
            DMA(LFd[t * 128:(t + 1) * 128, n * 1024:(n + 1) * 1024], G3[:, 0:1024])
            VTS(G1[:, 0:1024], G2[:, 0:1024], -1.0, 1.0, ALU.mult, ALU.add)
            DMA(K1d[t * 128:(t + 1) * 128, n * 1024:(n + 1) * 1024], G1[:, 0:1024])
        def ev_va(t, n, ps):
            VCP(B1[:, 0:1024], ps[:, :])
            DMA(VAd[t * 128:(t + 1) * 128, n * 1024:(n + 1) * 1024], B1[:, 0:1024])
        def mk_resid(dst, base):
            def ev(t, n, ps):
                DMA(G1[:, 0:1024], base[t * 128:(t + 1) * 128, n * 1024:(n + 1) * 1024])
                VTT(G2[:, 0:1024], G1[:, 0:1024], ps[:, :], ALU.add)
                DMA(dst[t * 128:(t + 1) * 128, n * 1024:(n + 1) * 1024], G2[:, 0:1024])
            return ev
        def ev_vb(t, n, ps):
            VCP(B1[:, 0:1024], ps[:, :])
            DMA(KVI[T + t * 128:T + (t + 1) * 128, n * 1024:(n + 1) * 1024], B1[:, 0:1024])
            pass

        LN(xin)
        tok_linear(XT, D, wqa, False, D, ev_qa)
        tok_linear(XT, D, wka, False, D, ev_fa)
        tok_linear(XT, D, wva, False, D, ev_va)

        VMS(SST[:, :], 0.0)
        VMS(SB16[:, :], 0.0)
        VMS(LPc[:, :], 0.0)
        for tt in range(NTT):
            r0 = tt * 128
            DMA(G1[:, :], QSd[r0:r0 + 128, :])
            DMA(G2[:, :], LFd[r0:r0 + 128, :])
            DMA(G3[:, :], K1d[r0:r0 + 128, :])
            DMA(B2[:, :], VAd[r0:r0 + 128, :])
            VCP(LFC[:, :], G2[64:128, :])
            VCP(VC1[:, :], B2[64:128, :])
            for c in range(2):
                for nn in range(4):
                    sl = slice(nn * 512, (nn + 1) * 512)
                    lfsrc = G2[0:64, sl] if c == 0 else LFC[0:64, sl]
                    TMM(PB4[c * 64:(c + 1) * 64, 0:512], cU64[:, :], lfsrc)
                    ACT(EP[c * 64:(c + 1) * 64, sl], PB4[c * 64:(c + 1) * 64, 0:512], AF.Exp)
            VTT(QD[:, :], G1[:, :], EP[:, :], ALU.mult)
            for c in range(2):
                for nn in range(4):
                    sl = slice(nn * 512, (nn + 1) * 512)
                    lfsrc = G2[0:64, sl] if c == 0 else LFC[0:64, sl]
                    TMM(PB4[c * 64:(c + 1) * 64, 0:512], cU64[:, :], lfsrc)
                    ACT(EN[c * 64:(c + 1) * 64, sl], PB4[c * 64:(c + 1) * 64, 0:512], AF.Exp, scale=-1.0)
            VTT(G3[:, :], G3[:, :], EN[:, :], ALU.mult)
            VCP(KD16[:, :], G3[:, :])
            for c in range(2):
                for nn in range(4):
                    sl = slice(nn * 512, (nn + 1) * 512)
                    lfsrc = G2[0:64, sl] if c == 0 else LFC[0:64, sl]
                    TMM(PB4[0:1, 0:512], cO64[:, :], lfsrc)
                    ACT(SROW[0:1, sl], PB4[0:1, 0:512], AF.Exp)
                    TMM(PB4[0:64, 0:512], cO1x64[:, :], SROW[0:1, sl])
                    VTT(KR[c * 64:(c + 1) * 64, sl], G3[c * 64:(c + 1) * 64, sl], PB4[0:64, 0:512], ALU.mult)
            for c in range(2):
                for h in range(16):
                    lfh = G2[0:64, h * 128:(h + 1) * 128] if c == 0 else LFC[0:64, h * 128:(h + 1) * 128]
                    TMM(PB4[:, c * 16 + h:c * 16 + h + 1], lfh, cO64[:, :])
            VCP(LAC[:, :], PB4[:, 0:32])
            ACT(EAC[:, :], LAC[:, :], AF.Exp)
            for c in range(2):
                cg = tt * 2 + c
                ACT(PCS[:, :], LPc[:, :], AF.Exp)
                DMA(LPCd[cg * 128:(cg + 1) * 128, :], PCS[:, :])
                VTT(LPc[:, :], LPc[:, :], LAC[:, c * 16:(c + 1) * 16], ALU.add)
            for h in range(16):
                hs = slice(h * 128, (h + 1) * 128)
                TTR(PT[:, :], QD[:, hs])
                VCP(QDT[:, hs], PT[:, :])
                TTR(PT[:, :], KD16[:, hs])
                VCP(KDT[:, hs], PT[:, :])
                DMA(QDTd[h * 128:(h + 1) * 128, r0:r0 + 128], QDT[:, hs])
            for h in range(16):
                hs = slice(h * 128, (h + 1) * 128)
                for c in range(2):
                    c0 = h * 128 + c * 64
                    TMM(PB4[0:64, c * 64:(c + 1) * 64], KDT[:, c0:c0 + 64], QDT[:, c0:c0 + 64])
                VTT(SCT[:, :], PB4[0:64, 0:128], cMASK[:, :], ALU.mult)
                for c in range(2):
                    c0 = h * 128 + c * 64
                    rs = slice(c * 64, (c + 1) * 64)
                    vsrc = B2[0:64, hs] if c == 0 else VC1[0:64, hs]
                    TMM(PBa[rs, 0:128], SCT[:, c * 64:(c + 1) * 64], vsrc,
                        start=True, stop=False)
                    TMM(PBa[rs, 0:128], QDT[:, c0:c0 + 64], SB16[:, hs], start=False, stop=True)
                    TMM(PBb[0:128, 0:128], KR[c * 64:(c + 1) * 64, hs], B2[c * 64:(c + 1) * 64, hs])
                    idx = c * 16 + h
                    VTS(T128[:, :], SST[:, hs], EAC[:, idx:idx + 1], None, ALU.mult)
                    VTT(SST[:, hs], T128[:, :], PBb[0:128, 0:128], ALU.add)
                    VCP(SB16[:, hs], SST[:, hs])
                VCP(OTI[:, hs], PBa[0:128, 0:128])
            DMA(OAd[r0:r0 + 128, :], OTI[:, :])
        VTS(G1[:, :], SST[:, :], cZSEL[:, 0:1], None, ALU.mult)
        DMA(fout[:, :], G1[:, :])
        DMA(G3[:, :], s0in[:, :])
        VTS(SST[:, :], G3[:, :], cWSEL[:, 0:1], None, ALU.mult)
        DMA(SROW[0:1, 0:2048], nwt[:, :])
        for nn in range(4):
            TMM(PB4[0:128, 0:512], cO1x128[:, :], SROW[0:1, nn * 512:(nn + 1) * 512])
            VCP(MNW[:, nn * 512:(nn + 1) * 512], PB4[0:128, 0:512])
        for tt in range(NTT):
            r0 = tt * 128
            DMA(OTI[:, :], OAd[r0:r0 + 128, :])
            for c in range(2):
                cg = tt * 2 + c
                DMA(PCS[:, :], LPCd[cg * 128:(cg + 1) * 128, :])
                for h in range(16):
                    hs = slice(h * 128, (h + 1) * 128)
                    VTS(B1[:, 0:128], SST[:, hs], PCS[:, h:h + 1], None, ALU.mult)
                    DMA(B2[:, 0:64], QDTd[h * 128:(h + 1) * 128, r0 + c * 64:r0 + c * 64 + 64])
                    TMM(PB4[c * 64:(c + 1) * 64, 0:128], B2[:, 0:64], B1[:, 0:128])
                    VTT(OTI[c * 64:(c + 1) * 64, hs], OTI[c * 64:(c + 1) * 64, hs],
                        PB4[c * 64:(c + 1) * 64, 0:128], ALU.add)
            SSQ(G2[:, :], OTI[:, :])
            VRS(STT[:, 0:1], G2[:, :])
            SMUL(STT[:, 1:2], STT[:, 0:1], 1.0 / D)
            SADD(STT[:, 2:3], STT[:, 1:2], 1e-6)
            SSQRT(STT[:, 3:4], STT[:, 2:3])
            VRC(STT[:, 4:5], STT[:, 3:4])
            VTS(G2[:, :], OTI[:, :], STT[:, 4:5], None, ALU.mult)
            VTT(B1[:, :], G2[:, :], MNW[:, :], ALU.mult)
            for kb in range(16):
                TTR(PT[:, :], B1[:, kb * 128:(kb + 1) * 128])
                VCP(B2[:, 128:256], PT[:, :])
                DMA(OATd[kb * 128:(kb + 1) * 128, r0:r0 + 128], B2[:, 128:256])
        tok_linear(OATd, D, woa, False, D, mk_resid(Rd, xin))

        def ffn(dst):
            LN(Rd)
            for kb in range(16):
                DMA(XTS[:, kb * 2048:(kb + 1) * 2048], XT[kb * 128:(kb + 1) * 128, :])
            for r in range(64):
                for kb in range(16):
                    DMA(WL[:, kb * 128:(kb + 1) * 128], WGUB[kb * 128:(kb + 1) * 128, r * 128:(r + 1) * 128])
                    DMA(WL2[:, kb * 128:(kb + 1) * 128],
                        WGUB[kb * 128:(kb + 1) * 128, 8192 + r * 128:8192 + (r + 1) * 128])
                for tc in range(2):
                    for kb in range(16):
                        rh = XTS[:, kb * 2048 + tc * 1024:kb * 2048 + (tc + 1) * 1024]
                        TMM(PBa[:, :], WL[:, kb * 128:(kb + 1) * 128], rh, start=(kb == 0), stop=(kb == 15))
                        TMM(PBb[:, :], WL2[:, kb * 128:(kb + 1) * 128], rh, start=(kb == 0), stop=(kb == 15))
                    ACT(G1[:, 0:1024], PBa[:, :], AF.Silu)
                    VTT(B1[:, 0:1024], G1[:, 0:1024], PBb[:, :], ALU.mult)
                    DMA(HTd[r * 128:(r + 1) * 128, tc * 1024:(tc + 1) * 1024], B1[:, 0:1024])
            tok_linear(HTd, 8192, WDNB, True, D, mk_resid(dst, Rd))
        ffn(Rd)

        LN(Rd)
        for kb in range(16):
            DMA(XTS[:, kb * 2048:(kb + 1) * 2048], XT[kb * 128:(kb + 1) * 128, :])
        def feat_qk(W, outd, scaleq, rs=False):
            for ot in range(16):
                for kb in range(16):
                    DMA(WS1[:, 0:128], W[kb * 128:(kb + 1) * 128, ot * 128:(ot + 1) * 128])
                    VCP(WL[:, kb * 128:(kb + 1) * 128], WS1[:, 0:128])
                for tc in range(2):
                    for kb in range(16):
                        TMM(PBa[:, :], WL[:, kb * 128:(kb + 1) * 128],
                            XTS[:, kb * 2048 + tc * 1024:kb * 2048 + (tc + 1) * 1024],
                            start=(kb == 0), stop=(kb == 15))
                    VCP(G1[:, tc * 1024:(tc + 1) * 1024], PBa[:, :])
                if scaleq:
                    SMUL(G1[:, :], G1[:, :], SCL)
                for hh in range(2):
                    b0 = hh * 64
                    VCP(G2[0:32, :], G1[b0:b0 + 32, :])
                    VCP(G3[0:32, :], G1[b0 + 32:b0 + 64, :])
                    VTT(EP[0:32, :], G2[0:32, :], cCOS[0:32, :], ALU.mult)
                    VTT(SST[0:32, :], G3[0:32, :], cSIN[0:32, :], ALU.mult)
                    VTT(B1[b0:b0 + 32, :], EP[0:32, :], SST[0:32, :], ALU.subtract)
                    VTT(EP[0:32, :], G2[0:32, :], cSIN[0:32, :], ALU.mult)
                    VTT(SST[0:32, :], G3[0:32, :], cCOS[0:32, :], ALU.mult)
                    VTT(B1[b0 + 32:b0 + 64, :], EP[0:32, :], SST[0:32, :], ALU.add)
                DMA(outd[ot * 128:(ot + 1) * 128, :], B1[:, :])
                if rs:
                    pass
        feat_qk(wqb, QTRd, True)
        feat_qk(wkb, KVI, False, rs=True)
        tok_linear(XT, D, wvb, False, D, ev_vb)
        for tt2 in range(32):
            DMA(kvout[tt2 * 128:(tt2 + 1) * 128, :], KVI[tt2 * 128:(tt2 + 1) * 128, :])
        for vh in range(16):
            for sb in range(32):
                src = KVI[T + sb * 128:T + (sb + 1) * 128, vh * 128:(vh + 1) * 128] if sb < 16 else \
                      kvpart[T + (sb - 16) * 128:T + (sb - 15) * 128, vh * 128:(vh + 1) * 128]
                DMA(XTS[:, sb * 128:(sb + 1) * 128], src)
            for tc in range(2):
                for j in range(2):
                    h2 = 2 * vh + j
                    DMA(XTS[0:64, 4096 + j * 1024:4096 + (j + 1) * 1024],
                        QTRd[h2 * 64:(h2 + 1) * 64, tc * 1024:(tc + 1) * 1024])
                    DMA(XTS[0:64, 6144:8192], KVI[h2 * 64:(h2 + 1) * 64, :])
                    DMA(XTS[0:64, 8192:10240], kvpart[h2 * 64:(h2 + 1) * 64, :])
                    PO = PBb if j == 0 else PBc
                    VMS(SROW[0:1, j * 1024:(j + 1) * 1024], 0.0)
                    for sb in range(32):
                        TMM(PBa[:, :], XTS[0:64, 6144 + sb * 128:6144 + (sb + 1) * 128],
                            XTS[0:64, 4096 + j * 1024:4096 + (j + 1) * 1024])
                        ACT(B1[:, 0:1024], PBa[:, :], AF.Exp)
                        TMM(PO[:, :], XTS[:, sb * 128:(sb + 1) * 128], B1[:, 0:1024],
                            start=(sb == 0), stop=(sb == 31))
                        for nn in range(2):
                            TMM(PB4[0:1, 0:512], cOB128[:, :], B1[:, nn * 512:(nn + 1) * 512])
                            VTT(SROW[0:1, j * 1024 + nn * 512:j * 1024 + (nn + 1) * 512],
                                SROW[0:1, j * 1024 + nn * 512:j * 1024 + (nn + 1) * 512],
                                PB4[0:1, 0:512], ALU.add)
                    VRC(RWS[0:1, j * 1024:(j + 1) * 1024], SROW[0:1, j * 1024:(j + 1) * 1024])
                for nn in range(2):
                    TMM(PBa[:, nn * 512:(nn + 1) * 512], cO1x128[:, :],
                        RWS[0:1, nn * 512:(nn + 1) * 512])
                VCP(G2[:, 0:1024], PBb[:, :])
                VTT(G1[:, 0:1024], G2[:, 0:1024], PBa[:, :], ALU.mult)
                for nn in range(2):
                    TMM(PBb[:, nn * 512:(nn + 1) * 512], cO1x128[:, :],
                        RWS[0:1, 1024 + nn * 512:1024 + (nn + 1) * 512])
                VCP(G3[:, 0:1024], PBc[:, :])
                VTT(G3[:, 0:1024], G3[:, 0:1024], PBb[:, :], ALU.mult)
                VTS(G2[:, 0:1024], G3[:, 0:1024], LAMC[:, 0:1], None, ALU.mult)
                VTT(OTI[:, tc * 1024:(tc + 1) * 1024], G1[:, 0:1024], G2[:, 0:1024], ALU.subtract)
            SSQ(G2[:, :], OTI[:, :])
            for nn in range(4):
                TMM(PB4[0:1, 0:512], cO128[:, :], G2[:, nn * 512:(nn + 1) * 512])
                SMUL(SROW[0:1, nn * 512:(nn + 1) * 512], PB4[0:1, 0:512], 1.0 / 128.0)
            SADD(SROW[0:1, 0:2048], SROW[0:1, 0:2048], 1e-5)
            SSQRT(RWS[0:1, 0:2048], SROW[0:1, 0:2048])
            VRC(SROW[0:1, 0:2048], RWS[0:1, 0:2048])
            for half in range(2):
                for nn in range(2):
                    TMM(PBa[:, nn * 512:(nn + 1) * 512], cO1x128[:, :],
                        SROW[0:1, half * 1024 + nn * 512:half * 1024 + (nn + 1) * 512])
                VTT(G1[:, 0:1024], OTI[:, half * 1024:(half + 1) * 1024], PBa[:, :], ALU.mult)
                VTS(B1[:, half * 1024:(half + 1) * 1024], G1[:, 0:1024], cSUB[:, 0:1], None, ALU.mult)
            DMA(OTBd[vh * 128:(vh + 1) * 128, :], B1[:, :])
        tok_linear(OTBd, D, wob, False, D, mk_resid(Rd, Rd))

        LN(Rd)
        for kb in range(16):
            DMA(XTS[:, kb * 2048:(kb + 1) * 2048], XT[kb * 128:(kb + 1) * 128, :])
        for r in range(64):
            for kb in range(16):
                DMA(WL[:, kb * 128:(kb + 1) * 128], WGUB[kb * 128:(kb + 1) * 128, r * 128:(r + 1) * 128])
                DMA(WL2[:, kb * 128:(kb + 1) * 128],
                    WGUB[kb * 128:(kb + 1) * 128, 8192 + r * 128:8192 + (r + 1) * 128])
            for tc in range(2):
                for kb in range(16):
                    rh = XTS[:, kb * 2048 + tc * 1024:kb * 2048 + (tc + 1) * 1024]
                    TMM(PBa[:, :], WL[:, kb * 128:(kb + 1) * 128], rh, start=(kb == 0), stop=(kb == 15))
                    TMM(PBb[:, :], WL2[:, kb * 128:(kb + 1) * 128], rh, start=(kb == 0), stop=(kb == 15))
                ACT(G1[:, 0:1024], PBa[:, :], AF.Silu)
                VTT(B1[:, 0:1024], G1[:, 0:1024], PBb[:, :], ALU.mult)
                DMA(HTd[r * 128:(r + 1) * 128, tc * 1024:(tc + 1) * 1024], B1[:, 0:1024])
        tok_linear(HTd, 8192, WDNB, True, D, mk_resid(yout, Rd))
        op('gp', lambda: nc.gpsimd.nop(), inc=1)

        with nc.Block() as block:
            def replay(engname):
                def body(eng):
                    for (e, fn, waits, si, inc) in OPS:
                        if e != engname:
                            continue
                        for (wi, wv) in waits:
                            if wv > 0:
                                eng.wait_ge(SEMS[wi], wv)
                        fn().then_inc(SEMS[si], inc)
                return body
            block.tensor(replay('te'))
            block.vector(replay('ve'))
            block.scalar(replay('se'))
            block.sync(replay('sy'))
            block.gpsimd(replay('gp'))

    CONSTVALS = [1.0 / 2048.0, -1.0, 1e-5, 1e-6, LAMBDA_INIT, 1.0 / 128.0, SCL, 1.0, 0.0, 0.5, 2048.0, 128.0]
    cvals_arr = np.zeros((128, 16), dtype=np.float32)
    for _i, _v in enumerate(CONSTVALS):
        cvals_arr[:, _i] = _v
    pos_all = np.arange(4096)
    in_maps = []
    for core in range(8):
        b = core // 2
        half = core % 2
        xs = np.ascontiguousarray(x[b, half * 2048:(half + 1) * 2048, :])
        pos = pos_all[half * 2048:(half + 1) * 2048]
        ang = np.outer(inv, pos)
        cosr = np.tile(np.cos(ang).astype(np.float32), (4, 1))
        sinr = np.tile(np.sin(ang).astype(np.float32), (4, 1))
        wsel = np.full((128, 1), 1.0 if half == 1 else 0.0, dtype=np.float32)
        pm = np.zeros((128, 8), dtype=np.float32)
        pm[:, core ^ 1] = 1.0
        pmb = pm.astype(ml_dtypes.bfloat16)
        zsel = np.full((128, 1), 1.0, dtype=np.float32)
        in_maps.append({
            "x": xs, "wqa": WQA, "wka": WKA, "wva": WVA, "woa": WOA, "nw": nw,
            "wqb": WQB, "wkb": WKB, "wvb": WVB, "wob": WOB,
            "lq1": lq1, "lq2": lq2, "lk1": lk1, "lk2": lk2, "subw": subw,
            "wgu": WGU, "wdn": WDN,
            "u64": U64, "o64": O64, "o1x64": O1x64, "o128": O128, "o1x128": O1x128,
            "ob128": OB128, "idb": IDB, "maskt": MASKT,
            "cosk": cosr, "sink": sinr, "wsel": wsel, "zsel": zsel,
            "cvals": cvals_arr, "pmask": pm, "pmaskb": pmb,
        })
    z_s0 = np.zeros((128, D), dtype=np.float32)
    z_kv = np.zeros((2 * T, D), dtype=ml_dtypes.bfloat16)
    for m in in_maps:
        m["s0in"] = z_s0
        m["kvpart"] = z_kv
    r1 = run_bass_kernel_spmd(nc, in_maps, core_ids=list(range(8)))
    for core in range(8):
        in_maps[core]["s0in"] = np.asarray(r1.results[core ^ 1]["fout"], dtype=np.float32)
        in_maps[core]["kvpart"] = np.asarray(r1.results[core ^ 1]["kvout"])
    r2 = run_bass_kernel_spmd(nc, in_maps, core_ids=list(range(8)))
    for core in range(8):
        in_maps[core]["kvpart"] = np.asarray(r2.results[core ^ 1]["kvout"])
    r3 = run_bass_kernel_spmd(nc, in_maps, core_ids=list(range(8)))
    out = np.empty((4, 4096, 2048), dtype=np.float32)
    for core in range(8):
        b = core // 2
        half = core % 2
        out[b, half * 2048:(half + 1) * 2048, :] = r3.results[core]["y"]
    return out
